# version 13
# speedup vs baseline: 1.1715x; 1.0110x over previous
import numpy as np
import concourse.bass as bass
import concourse.mybir as mybir
from concourse.bass_utils import run_bass_kernel_spmd
from contextlib import ExitStack

F32 = mybir.dt.float32
BF16 = mybir.dt.bfloat16
AF = mybir.ActivationFunctionType
ALU = mybir.AluOpType
AX = mybir.AxisListType

T, D, H, DH, FF, E = 2048, 1024, 16, 64, 4096, 8
NT, NCH, NFF = T // 128, D // 128, FF // 128
NEG = -1e9
QT_ORDER = [15, 11, 14, 10, 13, 9, 12, 8, 7, 6, 5, 4, 3, 2, 1, 0]
QT_BUF = ["A", "B"] * 8
PV_AT_POS = {6: 3, 7: 2, 11: 1, 15: 0}
ATTN_NHEADS = H
ATTN_SKIP = set()


def build_program(stage=99, dbg=()):
    nc = bass.Bass("TRN2", target_bir_lowering=False)

    x_d = nc.dram_tensor("x", [T, D], F32, kind="ExternalInput").ap()
    wqkv_d = nc.dram_tensor("wqkv", [D, 3 * D], F32, kind="ExternalInput").ap()
    bqkv_d = nc.dram_tensor("bqkv", [3 * D], F32, kind="ExternalInput").ap()
    wout_d = nc.dram_tensor("wout", [D, D], F32, kind="ExternalInput").ap()
    bout_d = nc.dram_tensor("bout", [D], F32, kind="ExternalInput").ap()
    rel_d = nc.dram_tensor("rel", [257, 16], F32, kind="ExternalInput").ap()
    lng_d = nc.dram_tensor("lng", [D], F32, kind="ExternalInput").ap()
    lnb_d = nc.dram_tensor("lnb", [D], F32, kind="ExternalInput").ap()
    wq_d = nc.dram_tensor("wq", [D, D], F32, kind="ExternalInput").ap()
    wk_d = nc.dram_tensor("wk", [D, D], F32, kind="ExternalInput").ap()
    wv_d = nc.dram_tensor("wv", [D, D], F32, kind="ExternalInput").ap()
    qs_d = nc.dram_tensor("qsc", [16, 1, 64], F32, kind="ExternalInput").ap()
    ks_d = nc.dram_tensor("ksc", [16, 1, 64], F32, kind="ExternalInput").ap()
    wo_d = nc.dram_tensor("wo", [D, D], F32, kind="ExternalInput").ap()
    gw_d = nc.dram_tensor("gw", [D, E], F32, kind="ExternalInput").ap()
    gb_d = nc.dram_tensor("gb", [E], F32, kind="ExternalInput").ap()
    w1_d = nc.dram_tensor("w1", [E, D, FF], F32, kind="ExternalInput").ap()
    b1_d = nc.dram_tensor("b1", [E, FF], F32, kind="ExternalInput").ap()
    w2_d = nc.dram_tensor("w2", [E, FF, D], F32, kind="ExternalInput").ap()
    b2_d = nc.dram_tensor("b2", [E, D], F32, kind="ExternalInput").ap()
    y_d = nc.dram_tensor("y", [T, D], F32, kind="ExternalOutput").ap()

    act_d = nc.dram_tensor("act_scr", [T, D], BF16, kind="Internal").ap()
    a_d = nc.dram_tensor("a_scr", [T, D], F32, kind="Internal").ap()
    probs_d = nc.dram_tensor("probs_scr", [H, T, T], BF16, kind="Internal").ap()
    relrep_d = nc.dram_tensor("relrep_scr", [128, 16], F32, kind="Internal").ap()
    s_d = nc.dram_tensor("s_scr", [1, 6160], F32, kind="Internal").ap()
    w1b_d = nc.dram_tensor("w1b_scr", [E, D, FF], BF16, kind="Internal").ap()
    w2b_d = nc.dram_tensor("w2b_scr", [E, FF, D], BF16, kind="Internal").ap()
    win_d = nc.dram_tensor("win_scr", [257, 2048], F32, kind="Internal").ap()

    rel_flat = rel_d.rearrange("a b -> (a b)").rearrange("(o f) -> o f", o=1)
    relrep_flat = relrep_d.rearrange("a b -> (a b)").rearrange("(o f) -> o f", o=1)
    s2 = s_d.rearrange("o (a b) -> (o a) b", b=16)

    dbg_outs = {}
    _dbg_spec = {"n0": ([T, D], BF16), "a": ([T, D], F32), "h2": ([T, D], BF16),
                 "ao": ([128, NCH * T], BF16), "q": ([128, NCH * T], BF16),
                 "k": ([128, NCH * T], BF16), "v": ([128, NT * D], BF16),
                 "hq": ([T, D], BF16), "nf": ([128, NCH * T], BF16),
                 "q2": ([128, NCH * T], BF16), "k2": ([128, NCH * T], BF16),
                 "v2": ([128, NT * D], BF16), "ao2": ([128, NCH * T], BF16),
                 "gt": ([128, 64], F32), "oh": ([128, 2 * E * D], F32)}
    for name in dbg:
        shp, dt = _dbg_spec[name]
        dbg_outs[name] = nc.dram_tensor("dbg_" + name, shp, dt,
                                        kind="ExternalOutput").ap()

    es = ExitStack()
    with es:
        sbuf = lambda n, s, d: es.enter_context(nc.sbuf_tensor(n, s, d))
        SEM = {n: es.enter_context(nc.semaphore(n))
               for n in ("ts", "vs", "ss", "gs")}
        C = {n: 0 for n in SEM}
        DS = {}

        def inc(sem, instr, n=1):
            C[sem] += n
            instr.then_inc(SEM[sem], n)
            return C[sem]

        def W(eng, sem, tok):
            if tok > 0:
                eng.wait_ge(SEM[sem], tok)

        def ddma(name, ins):
            if name not in DS:
                DS[name] = [es.enter_context(nc.semaphore(name)), 0]
            s = DS[name]
            s[1] += 16
            ins.then_inc(s[0], 16)
            return (s[0], s[1])

        def WD(eng, tok):
            if tok is not None:
                eng.wait_ge(tok[0], tok[1])

        H_ = {"inc": inc, "W": W, "ddma": ddma, "WD": WD, "SEM": SEM, "C": C,
              "DS": DS}

        block = es.enter_context(nc.Block())

        ident = sbuf("ident", [128, 128], F32)
        flip = sbuf("flip", [128, 128], BF16)
        mc_add = sbuf("mc_add", [128, 128], BF16)
        onesf = sbuf("onesf", [1, 128], F32)
        g_bc = sbuf("g_bc", [128, D], F32)
        b_bc = sbuf("b_bc", [128, D], F32)
        bqkv_fm = sbuf("bqkv_fm", [128, 24], F32)
        bq_scaled = sbuf("bq_scaled", [128, 8], F32)
        far_c = sbuf("far_c", [128, 16], F32)
        wqk_fm = sbuf("wqk_fm", [128, 8], F32)
        e_tiles = sbuf("e_tiles", [128, NCH * 16], F32)
        b1_fm = sbuf("b1_fm", [128, E * 32], F32)
        b2_sb = sbuf("b2_sb", [8, D], F32)
        gb_bc = sbuf("gb_bc", [128, 8], F32)
        bout_bc = sbuf("bout_bc", [128, D], F32)
        bv_bc = sbuf("bv_bc", [128, D], F32)
        smalls = sbuf("smalls", [128, 64], F32)

        @block.gpsimd
        def _(g):
            g.memset(ident[:], 1.0)
            g.memset(flip[:], 1.0)
            g.memset(mc_add[:], 0.0)
            g.memset(onesf[:], 1.0)
            g.memset(smalls[:, 33:34], 1e-5)
            g.memset(smalls[:, 34:35], 1e-12)
            g.memset(smalls[:, 35:36], 10.0)
            ins = g.memset(e_tiles[:], 0.0)
            tk = inc("gs", ins)
            g.wait_ge(SEM["gs"], tk)
            g.affine_select(out=ident[:], in_=ident[:], pattern=[[-1, 128]],
                            compare_op=ALU.is_equal, fill=0.0, base=0,
                            channel_multiplier=1)
            g.affine_select(out=flip[:], in_=flip[:], pattern=[[1, 128]],
                            compare_op=ALU.is_equal, fill=0.0, base=-127,
                            channel_multiplier=1)
            g.affine_select(out=mc_add[:], in_=mc_add[:], pattern=[[-1, 128]],
                            compare_op=ALU.is_ge, fill=NEG, base=0,
                            channel_multiplier=1)
            for c in range(NCH):
                g.memset(e_tiles[0:64, c * 16 + 2 * c: c * 16 + 2 * c + 1], 1.0)
                g.memset(e_tiles[64:128, c * 16 + 2 * c + 1: c * 16 + 2 * c + 2], 1.0)
        tok_const = C["gs"]

        with nc.sbuf_tensor("prows", [1, 4 * D + 32], F32) as prows, \
             nc.sbuf_tensor("pstage", [32, 1408], F32) as pstage, \
             nc.psum_tensor("ps_p0", [128, 1024], F32) as ps_p0:
            R_BOUT, R_LNG, R_LNB, R_BV = 0, D, 2 * D, 3 * D
            R_GB, R_REL0 = 4 * D, 4 * D + 16

            @block.sync
            def _(sy):
                ddma("p0r", sy.dma_start(prows[0:1, R_BOUT:R_BOUT + D],
                     bout_d.rearrange("(o f) -> o f", o=1)))
                ddma("p0r", sy.dma_start(prows[0:1, R_LNG:R_LNG + D],
                     lng_d.rearrange("(o f) -> o f", o=1)))
                ddma("p0r", sy.dma_start(prows[0:1, R_LNB:R_LNB + D],
                     lnb_d.rearrange("(o f) -> o f", o=1)))
                ddma("p0r", sy.dma_start(prows[0:1, R_BV:R_BV + D],
                     bqkv_d.rearrange("(o f) -> o f", o=1)[0:1, 2 * D:3 * D]))
                ddma("p0r", sy.dma_start(prows[0:1, R_GB:R_GB + 8],
                     gb_d.rearrange("(o f) -> o f", o=1)))
                ddma("p0r", sy.dma_start(prows[0:1, R_REL0:R_REL0 + 16],
                     rel_flat[0:1, 0:16]))
            tok_rows = (DS["p0r"][0], DS["p0r"][1])

            @block.gpsimd
            def _(g):
                ddma("p0s", g.dma_start(pstage[0:24, 0:128],
                     bqkv_d.rearrange("(a b) -> a b", b=128)[0:24, :]))
                ddma("p0s", g.dma_start(pstage[0:8, 128:256],
                     qs_d.rearrange("h o d -> (h o d)")
                     .rearrange("(a b) -> a b", b=128)))
                ddma("p0s", g.dma_start(pstage[0:8, 256:384],
                     ks_d.rearrange("h o d -> (h o d)")
                     .rearrange("(a b) -> a b", b=128)))
                ddma("p0s", g.dma_start(b2_sb[:], b2_d))
                for e in range(E):
                    ddma("p0s", g.dma_start(
                        pstage[0:32, 384 + e * 128: 384 + (e + 1) * 128],
                        b1_d.rearrange("e (a b) -> e a b", b=128)[e]))
            tok_pstage = (DS["p0s"][0], DS["p0s"][1])

            @block.tensor
            def _(t):
                W(t, "gs", tok_const)
                WD(t, tok_pstage)
                t.transpose(ps_p0[:, 0:24], pstage[0:24, 0:128], ident[0:24, 0:24])
                t.transpose(ps_p0[:, 24:32], pstage[0:8, 128:256], ident[0:8, 0:8])
                t.transpose(ps_p0[:, 32:40], pstage[0:8, 256:384], ident[0:8, 0:8])
                ins = None
                for e in range(E):
                    ins = t.transpose(ps_p0[:, 48 + e * 32: 48 + (e + 1) * 32],
                                      pstage[0:32, 384 + e * 128: 384 + (e + 1) * 128],
                                      ident[0:32, 0:32])
                inc("ts", ins)
            tok_t0 = C["ts"]

            @block.vector
            def _(v):
                W(v, "ts", tok_t0)
                v.tensor_copy(bqkv_fm[:], ps_p0[:, 0:24])
                v.tensor_scalar(bq_scaled[:], ps_p0[:, 0:8], 0.125, None, ALU.mult)
                ins = v.tensor_copy(wqk_fm[:], ps_p0[:, 24:32])
                tk = inc("vs", ins)
                v.wait_ge(SEM["vs"], tk)
                v.tensor_tensor(wqk_fm[:], wqk_fm[:], ps_p0[:, 32:40], ALU.mult)
                ins = v.tensor_copy(b1_fm[:], ps_p0[:, 48:48 + 256])
                inc("vs", ins)
            tok_v0 = C["vs"]

            def bcast(dst, row_off, n, extra_wait_vs):
                @block.tensor
                def _(t):
                    W(t, "vs", extra_wait_vs)
                    WD(t, tok_rows)
                    ins = None
                    for j in range(0, n, 512):
                        w = min(512, n - j)
                        ins = t.matmul(ps_p0[:, j:j + w], onesf[:],
                                       prows[0:1, row_off + j:row_off + j + w],
                                       start=True, stop=True, skip_group_check=True)
                    inc("ts", ins)
                tk = C["ts"]

                @block.vector
                def _(v):
                    W(v, "ts", tk)
                    ins = v.tensor_copy(dst[:, 0:n], ps_p0[:, 0:n])
                    inc("vs", ins)
                return C["vs"]

            last = bcast(bout_bc, R_BOUT, D, tok_v0)
            last = bcast(g_bc, R_LNG, D, last)
            last = bcast(b_bc, R_LNB, D, last)
            last = bcast(bv_bc, R_BV, D, last)
            last = bcast(gb_bc, R_GB, 8, last)
            last = bcast(far_c, R_REL0, 16, last)
            tok_p0_done = last

            _rr = []

            @block.sync
            def _(sy):
                W(sy, "vs", tok_p0_done)
                _rr.append(ddma("rrp", sy.dma_start(relrep_d, far_c[:, 0:16])))
            tok_relrep = _rr[0]

        with ExitStack() as esA:
            sbA = lambda n, s, d: esA.enter_context(nc.sbuf_tensor(n, s, d))
            esAO = ExitStack()
            esA.callback(esAO.close)
            ao_fm = esAO.enter_context(nc.sbuf_tensor("ao_fm", [128, NCH * T], BF16))
            esQ = ExitStack()
            esA.callback(esQ.close)
            sbQ = lambda n, s, d: esQ.enter_context(nc.sbuf_tensor(n, s, d))
            q_fm = sbQ("q_fm", [128, NCH * T], BF16)
            k_fm = sbQ("k_fm", [128, NCH * T], BF16)
            v_tm = sbQ("v_tm", [128, NT * D], BF16)
            dtl = sbQ("dtl", [128, H * 128], BF16)
            ntl = sbQ("ntl", [128, H * 128], BF16)

            @block.gpsimd
            def _(g):
                WD(g, tok_relrep)
                ddma("sS", g.dma_start(s_d[0:1, 0:2048], relrep_flat))
                ddma("sS", g.dma_start(s_d[0:1, 2048:6160], rel_flat))
                WD(g, (DS["sS"][0], DS["sS"][1]))
                for dk in range(128):
                    ddma("winq", g.dma_start(
                        win_d[1:257, dk * 16:(dk + 1) * 16],
                        s2[1 + dk:257 + dk, :]))
            tok_win = (DS["winq"][0], DS["winq"][1])

            esN = ExitStack()
            esA.callback(esN.close)
            n0_fm = esN.enter_context(nc.sbuf_tensor("n0_fm", [128, NCH * T], BF16))
            bigd_r = ao_fm[:, 0:2048]
            bign_r = ao_fm[:, 2048:4096]

            tok_n0fm, tok_hw_last = emit_ln_to_fm(
                nc, block, H_, smalls, src_rows=x_d, fm_dst=n0_fm,
                dram_buf=act_d, tag="n0",
                first_waits_e=[("vs", tok_p0_done)])

            if "n0" in dbg_outs:
                @block.gpsimd
                def _(g):
                    WD(g, tok_hw_last[0])
                    WD(g, tok_hw_last[1])
                    ddma("dbg", g.dma_start(dbg_outs["n0"], act_d))
            if "nf" in dbg_outs:
                @block.gpsimd
                def _(g):
                    WD(g, tok_n0fm)
                    ddma("dbg", g.dma_start(dbg_outs["nf"], n0_fm[:]))

            if stage < 2:
                _finish(block, SEM, C, W, DS)
                return nc, dbg_outs

            with nc.sbuf_tensor("wtile", [128, 2 * 1024], BF16) as wtile, \
                 nc.sbuf_tensor("wv_sb", [128, NCH * 512], BF16) as wv_sb, \
                 nc.psum_tensor("ps_pa", [128, 512], F32) as ps_pa, \
                 nc.psum_tensor("ps_pb", [128, 512], F32) as ps_pb:
                ps_proj = [ps_pa, ps_pb]
                tok_wt = [None] * 16
                tok_mm_qk = {}
                tok_evac_qk = {}
                for dc in range(16):
                    if dc == 5:
                        tok_dtiles = emit_bias_build(
                            nc, block, H_, DS, tok_win, win_d, dtl, ntl,
                            far_c, mc_add, flip, bigd_r, bign_r,
                            tok_p0_done, tok_const, tok_hw_last)

                    @block.gpsimd
                    def _(g, dc=dc):
                        if dc == 0:
                            WD(g, tok_hw_last[0])
                            WD(g, tok_hw_last[1])
                        if dc >= 2:
                            W(g, "ts", tok_mm_qk[(dc - 2, 3)])
                        src = wqkv_d.rearrange("(dk p) f -> p dk f", p=128)[
                            :, :, dc * 128:(dc + 1) * 128]
                        dst = wtile[:, (dc % 2) * 1024:(dc % 2 + 1) * 1024] \
                            .rearrange("p (dk f) -> p dk f", f=128)
                        tok_wt[dc] = ddma("wt%d" % (dc % 2), g.dma_start(dst, src))

                    for tq in range(4):
                        u = dc * 4 + tq

                        @block.tensor
                        def _(t, dc=dc, tq=tq, u=u):
                            WD(t, tok_wt[dc])
                            if u == 0:
                                WD(t, tok_n0fm)
                            if u >= 2:
                                W(t, "vs", tok_evac_qk[u - 2])
                            ins = None
                            for dk in range(NCH):
                                ins = t.matmul(
                                    ps_proj[u % 2][:],
                                    wtile[:, (dc % 2) * 1024 + dk * 128:
                                          (dc % 2) * 1024 + (dk + 1) * 128],
                                    n0_fm[:, dk * T + tq * 512:dk * T + tq * 512 + 512],
                                    start=(dk == 0), stop=(dk == NCH - 1))
                            tok_mm_qk[(dc, tq)] = inc("ts", ins)

                        @block.vector
                        def _(v, dc=dc, tq=tq, u=u):
                            W(v, "ts", tok_mm_qk[(dc, tq)])
                            if dc < 8:
                                dstf = q_fm[:, dc * T + tq * 512:dc * T + tq * 512 + 512]
                                ins = v.tensor_scalar(dstf, ps_proj[u % 2][:],
                                                      0.125, bq_scaled[:, dc:dc + 1],
                                                      ALU.mult, ALU.add)
                            else:
                                dstf = k_fm[:, (dc - 8) * T + tq * 512:
                                            (dc - 8) * T + tq * 512 + 512]
                                ins = v.tensor_scalar(dstf, ps_proj[u % 2][:],
                                                      bqkv_fm[:, dc:dc + 1], None,
                                                      ALU.add)
                            tok_evac_qk[u] = inc("vs", ins)
                tok_qk_done = C["vs"]

                tok_mm_v = {}
                tok_evac_v = {}
                for dh2 in range(2):
                    @block.gpsimd
                    def _(g, dh2=dh2):
                        if dh2 == 1:
                            W(g, "ts", tok_mm_v[(0, NT - 1)])
                        for dk in range(NCH):
                            ddma("wv%d" % dh2, g.dma_start(
                                wv_sb[:, dk * 512:(dk + 1) * 512],
                                wqkv_d[dk * 128:(dk + 1) * 128,
                                       2048 + dh2 * 512:2048 + dh2 * 512 + 512]))
                    tok_wv = (DS["wv%d" % dh2][0], DS["wv%d" % dh2][1])
                    for tt in range(NT):
                        u = dh2 * NT + tt

                        @block.tensor
                        def _(t, tt=tt, dh2=dh2, u=u, tok_wv=tok_wv):
                            WD(t, tok_wv)
                            if u < 2:
                                W(t, "vs", tok_qk_done)
                            if u >= 2:
                                W(t, "vs", tok_evac_v[u - 2])
                            ins = None
                            for dk in range(NCH):
                                ins = t.matmul(
                                    ps_proj[u % 2][:],
                                    n0_fm[:, dk * T + tt * 128:dk * T + (tt + 1) * 128],
                                    wv_sb[:, dk * 512:(dk + 1) * 512],
                                    start=(dk == 0), stop=(dk == NCH - 1))
                            tok_mm_v[(dh2, tt)] = inc("ts", ins)
                            tok_mm_v[u] = tok_mm_v[(dh2, tt)]

                        @block.vector
                        def _(v, tt=tt, dh2=dh2, u=u):
                            W(v, "ts", tok_mm_v[u])
                            ins = v.tensor_tensor(
                                v_tm[:, tt * 1024 + dh2 * 512:tt * 1024 + dh2 * 512 + 512],
                                ps_proj[u % 2][:],
                                bv_bc[:, dh2 * 512:(dh2 + 1) * 512], ALU.add)
                            tok_evac_v[u] = inc("vs", ins)
                tok_v_done = C["vs"]

            for nm, src in (("q", q_fm), ("k", k_fm), ("v", v_tm)):
                if nm in dbg_outs:
                    @block.gpsimd
                    def _(g, nm=nm, src=src):
                        W(g, "ss", tok_qk_done)
                        W(g, "vs", tok_v_done)
                        ddma("dbg", g.dma_start(dbg_outs[nm], src[:]))

            esN.close()

            if stage < 3:
                _finish(block, SEM, C, W, DS)
                return nc, dbg_outs

            tok_pv_done, tok_att_ts = emit_attention(
                nc, block, H_, smalls, q_fm, k_fm, v_tm,
                ao_fm, probs_d, dtl, ntl, far_c, mc_add, use_rel=True, tag="a3",
                first_waits_mm=[("vs", tok_qk_done), ("vs", tok_v_done)],
                first_waits_v=[("vs", tok_dtiles)],
                bg_chunks=[(w2b_d[e, c * 128:(c + 1) * 128, :],
                            w2_d[e, c * 128:(c + 1) * 128, :])
                           for e in range(E) for c in range(32)],
                nheads=ATTN_NHEADS)

            if "ao" in dbg_outs:
                @block.gpsimd
                def _(g):
                    W(g, "vs", tok_pv_done)
                    ddma("dbg", g.dma_start(dbg_outs["ao"], ao_fm[:]))

            if stage < 4:
                _finish(block, SEM, C, W, DS)
                return nc, dbg_outs

            esQ.close()

            GA = 4
            with nc.sbuf_tensor("wout_sb", [128, NCH * 1024], BF16) as wout_sb, \
                 nc.sbuf_tensor("a4_xr", [128, GA * D], F32) as xr_st, \
                 nc.sbuf_tensor("a4_o", [128, GA * D], F32) as o4_st, \
                 nc.sbuf_tensor("a4_a", [128, GA * D], F32) as a4_st, \
                 nc.sbuf_tensor("a4_t", [128, GA * D], F32) as t4_st, \
                 nc.sbuf_tensor("a4_hq", [128, GA * D], BF16) as hq_st, \
                 nc.sbuf_tensor("a4_bn", [128, GA * 12], F32) as bn4, \
                 nc.sbuf_tensor("a4_mv", [128, GA * 2], F32) as mv4, \
                 nc.psum_tensor("a4_p0", [128, 1024], F32) as a4p0, \
                 nc.psum_tensor("a4_p1", [128, 1024], F32) as a4p1:
                ps4 = [a4p0, a4p1]

                @block.gpsimd
                def _(g):
                    W(g, "ts", tok_att_ts)
                    for rr in range(4):
                        nm = "a3pw%d" % rr
                        if nm in DS:
                            g.wait_ge(DS[nm][0], DS[nm][1])
                    for dk in range(NCH):
                        ddma("wouts", g.dma_start(
                            wout_sb[:, dk * 1024:(dk + 1) * 1024],
                            wout_d[dk * 128:(dk + 1) * 128, :]))
                tok_wouts = (DS["wouts"][0], DS["wouts"][1])

                tok_x_grp = None
                tok_aff = {}
                tok_sd1, tok_sd2, tok_sd3 = {}, {}, {}
                tok_aggr = {}
                tok_tap = {}
                tok_hq = {}
                tok_oev = {}
                tok_amr = {}
                tok_rn1, tok_ap1, tok_rn3, tok_ap3 = {}, {}, {}, {}
                prev_aff_last = 0
                prev_tap_last = 0
                prev_amr_last = 0
                tok_aw_grp = None
                tok_hw_grp = None

                def _cols(i):
                    return (smalls[:, 16 + i:17 + i], smalls[:, 20 + i:21 + i],
                            smalls[:, 24 + i:25 + i])

                for g0 in range(0, NT, GA):
                    tiles = list(range(g0, g0 + GA))

                    @block.sync
                    def _(sy, tiles=tiles, pv=prev_aff_last):
                        if pv:
                            W(sy, "gs", pv)
                        for i, tt in enumerate(tiles):
                            ddma("a4x", sy.dma_start(
                                xr_st[:, i * D:(i + 1) * D],
                                x_d[tt * 128:(tt + 1) * 128, :]))
                    tok_x_grp = (DS["a4x"][0], DS["a4x"][1])

                    for i, tt in enumerate(tiles):
                        tok_mm4_box = []

                        @block.tensor
                        def _(t, i=i, tt=tt, g0=g0):
                            if g0 == 0 and i == 0:
                                WD(t, tok_wouts)
                                W(t, "vs", tok_pv_done)
                            if (g0 + i) >= 2:
                                W(t, "vs", tok_oev[g0 + i - 2])
                            ins = None
                            for dh2 in range(2):
                                for dk in range(NCH):
                                    ins = t.matmul(
                                        ps4[i % 2][:, dh2 * 512:dh2 * 512 + 512],
                                        ao_fm[:, dk * T + tt * 128:dk * T + (tt + 1) * 128],
                                        wout_sb[:, dk * 1024 + dh2 * 512:
                                                dk * 1024 + dh2 * 512 + 512],
                                        start=(dk == 0), stop=(dk == NCH - 1),
                                        skip_group_check=True)
                            tok_mm4_box.append(inc("ts", ins))

                        @block.vector
                        def _(v, i=i, tt=tt, g0=g0, tok_mm4=tok_mm4_box[0] if tok_mm4_box else None):
                            W(v, "ts", tok_mm4_box[0])
                            if (g0 + i) >= GA:
                                W(v, "vs", tok_hq.get(g0 + i - GA, 0))
                            ins2 = v.tensor_tensor(o4_st[:, i * D:(i + 1) * D],
                                                   ps4[i % 2][:], bout_bc[:],
                                                   ALU.add)
                            tok_oev[g0 + i] = inc("vs", ins2)

                    @block.vector
                    def _(v, tiles=tiles, g0=g0):
                        for i in range(GA):
                            W(v, "vs", tok_oev[g0 + i])
                            v.bn_stats(bn4[:, i * 12:i * 12 + 6],
                                       o4_st[:, i * D:i * D + 512])
                            ins = v.bn_stats(bn4[:, i * 12 + 6:i * 12 + 12],
                                             o4_st[:, i * D + 512:i * D + 1024])
                            tk = inc("vs", ins)
                            v.wait_ge(SEM["vs"], tk)
                            ins = v.bn_aggr(mv4[:, i * 2:i * 2 + 2],
                                            bn4[:, i * 12:i * 12 + 12])
                            tok_aggr[(g0, 1, i)] = inc("vs", ins)

                    @block.scalar
                    def _(s, g0=g0):
                        for i in range(GA):
                            sd, rstd, nmr = _cols(i)
                            W(s, "vs", tok_aggr[(g0, 1, i)])
                            ins = s.activation(sd, mv4[:, i * 2 + 1:i * 2 + 2],
                                               AF.Sqrt, bias=smalls[:, 33:34])
                            tok_sd1[(g0, i)] = inc("ss", ins)

                    @block.vector
                    def _(v, g0=g0):
                        for i in range(GA):
                            sd, rstd, nmr = _cols(i)
                            W(v, "ss", tok_sd1[(g0, i)])
                            ins = v.reciprocal(rstd, sd)
                            tk = inc("vs", ins)
                            v.wait_ge(SEM["vs"], tk)
                            ins = v.tensor_scalar(nmr, rstd, mv4[:, i * 2:i * 2 + 1],
                                                  -1.0, ALU.mult, ALU.mult)
                            tok_rn1[(g0, i)] = inc("vs", ins)

                    @block.scalar
                    def _(s, g0=g0):
                        for i in range(GA):
                            sd, rstd, nmr = _cols(i)
                            W(s, "vs", tok_rn1[(g0, i)])
                            ins = s.activation(o4_st[:, i * D:(i + 1) * D],
                                               o4_st[:, i * D:(i + 1) * D],
                                               AF.Identity, bias=nmr, scale=rstd)
                            tok_ap1[(g0, i)] = inc("ss", ins)

                    @block.gpsimd
                    def _(g, g0=g0, pv_tap=prev_tap_last, pv_aw=tok_aw_grp,
                          tok_x_grp=tok_x_grp):
                        for i in range(GA):
                            W(g, "ss", tok_ap1[(g0, i)])
                            if i == 0:
                                WD(g, tok_x_grp)
                                if pv_tap:
                                    W(g, "ss", pv_tap)
                                if pv_aw is not None:
                                    WD(g, pv_aw)
                            ins = g.tensor_tensor(a4_st[:, i * D:(i + 1) * D],
                                                  o4_st[:, i * D:(i + 1) * D],
                                                  xr_st[:, i * D:(i + 1) * D],
                                                  ALU.add)
                            tok_aff[(g0, i)] = inc("gs", ins)
                    prev_aff_last = tok_aff[(g0, GA - 1)]

                    @block.sync
                    def _(sy, tiles=tiles, g0=g0):
                        for i, tt in enumerate(tiles):
                            W(sy, "gs", tok_aff[(g0, i)])
                            ddma("a4aw", sy.dma_start(
                                a_d[tt * 128:(tt + 1) * 128, :],
                                a4_st[:, i * D:(i + 1) * D]))
                    tok_aw_grp = (DS["a4aw"][0], DS["a4aw"][1])

                    @block.vector
                    def _(v, g0=g0):
                        for i in range(GA):
                            W(v, "ss", tok_sd1[(g0, i)])
                            W(v, "gs", tok_aff[(g0, i)])
                            v.bn_stats(bn4[:, i * 12:i * 12 + 6],
                                       a4_st[:, i * D:i * D + 512])
                            ins = v.bn_stats(bn4[:, i * 12 + 6:i * 12 + 12],
                                             a4_st[:, i * D + 512:i * D + 1024])
                            tk = inc("vs", ins)
                            v.wait_ge(SEM["vs"], tk)
                            ins = v.bn_aggr(mv4[:, i * 2:i * 2 + 2],
                                            bn4[:, i * 12:i * 12 + 12])
                            tok_aggr[(g0, 2, i)] = inc("vs", ins)

                    @block.scalar
                    def _(s, g0=g0):
                        for i in range(GA):
                            sd, rstd, nmr = _cols(i)
                            W(s, "vs", tok_aggr[(g0, 2, i)])
                            ins = s.activation(sd, mv4[:, i * 2 + 1:i * 2 + 2],
                                               AF.Sqrt, bias=smalls[:, 33:34])
                            tok_sd2[(g0, i)] = inc("ss", ins)

                    @block.vector
                    def _(v, g0=g0):
                        for i in range(GA):
                            sd, rstd, nmr = _cols(i)
                            W(v, "ss", tok_sd2[(g0, i)])
                            ins = v.reciprocal(rstd, sd)
                            tk = inc("vs", ins)
                            v.wait_ge(SEM["vs"], tk)
                            ins = v.tensor_scalar(nmr, rstd, mv4[:, i * 2:i * 2 + 1],
                                                  -1.0, ALU.mult, ALU.mult)
                            inc("vs", ins)
                    tok_nm2_last = C["vs"]

                    @block.scalar
                    def _(s, g0=g0, tok_nm2_last=tok_nm2_last, pv_amr=prev_amr_last):
                        for i in range(GA):
                            sd, rstd, nmr = _cols(i)
                            W(s, "vs", tok_nm2_last)
                            if i == 0 and pv_amr:
                                W(s, "vs", pv_amr)
                            ins = s.activation(t4_st[:, i * D:(i + 1) * D],
                                               a4_st[:, i * D:(i + 1) * D],
                                               AF.Identity, bias=nmr, scale=rstd)
                            tok_tap[(g0, i)] = inc("ss", ins)
                    prev_tap_last = tok_tap[(g0, GA - 1)]

                    @block.vector
                    def _(v, g0=g0):
                        for i in range(GA):
                            W(v, "ss", tok_tap[(g0, i)])
                            v.bn_stats(bn4[:, i * 12:i * 12 + 6],
                                       t4_st[:, i * D:i * D + 512])
                            ins = v.bn_stats(bn4[:, i * 12 + 6:i * 12 + 12],
                                             t4_st[:, i * D + 512:i * D + 1024])
                            tk = inc("vs", ins)
                            v.wait_ge(SEM["vs"], tk)
                            ins = v.bn_aggr(mv4[:, i * 2:i * 2 + 2],
                                            bn4[:, i * 12:i * 12 + 12])
                            tok_aggr[(g0, 3, i)] = inc("vs", ins)

                    @block.scalar
                    def _(s, g0=g0):
                        for i in range(GA):
                            sd, rstd, nmr = _cols(i)
                            W(s, "vs", tok_aggr[(g0, 3, i)])
                            ins = s.activation(sd, mv4[:, i * 2 + 1:i * 2 + 2],
                                               AF.Sqrt, bias=smalls[:, 33:34])
                            tok_sd3[(g0, i)] = inc("ss", ins)

                    @block.vector
                    def _(v, g0=g0):
                        for i in range(GA):
                            sd, rstd, nmr = _cols(i)
                            W(v, "ss", tok_sd3[(g0, i)])
                            ins = v.reciprocal(rstd, sd)
                            tk = inc("vs", ins)
                            v.wait_ge(SEM["vs"], tk)
                            ins = v.tensor_scalar(nmr, rstd, mv4[:, i * 2:i * 2 + 1],
                                                  -1.0, ALU.mult, ALU.mult)
                            tok_rn3[(g0, i)] = inc("vs", ins)

                    @block.scalar
                    def _(s, g0=g0):
                        for i in range(GA):
                            sd, rstd, nmr = _cols(i)
                            W(s, "vs", tok_rn3[(g0, i)])
                            W(s, "gs", tok_aff[(g0, i)])
                            ins = s.activation(o4_st[:, i * D:(i + 1) * D],
                                               t4_st[:, i * D:(i + 1) * D],
                                               AF.Identity, bias=nmr, scale=rstd)
                            tok_ap3[(g0, i)] = inc("ss", ins)

                    @block.vector
                    def _(v, g0=g0, pv_hw=tok_hw_grp):
                        for i in range(GA):
                            W(v, "ss", tok_ap3[(g0, i)])
                            ins = v.tensor_tensor(o4_st[:, i * D:(i + 1) * D],
                                                  o4_st[:, i * D:(i + 1) * D],
                                                  g_bc[:], ALU.mult)
                            tk = inc("vs", ins)
                            tok_amr[(g0, i)] = tk
                            v.wait_ge(SEM["vs"], tk)
                            if i == 0 and pv_hw is not None:
                                WD(v, pv_hw)
                            ins = v.tensor_tensor(hq_st[:, i * D:(i + 1) * D],
                                                  o4_st[:, i * D:(i + 1) * D],
                                                  b_bc[:], ALU.add)
                            tok_hq[g0 + i] = inc("vs", ins)
                    prev_amr_last = tok_amr[(g0, GA - 1)]

                    @block.sync
                    def _(sy, tiles=tiles, g0=g0):
                        for i, tt in enumerate(tiles):
                            W(sy, "vs", tok_hq[g0 + i])
                            if g0 == 0 and i == 0:
                                WD(sy, tok_n0fm)
                            ddma("a4hw", sy.dma_start(
                                act_d[tt * 128:(tt + 1) * 128, :],
                                hq_st[:, i * D:(i + 1) * D]))
                    tok_hw_grp = (DS["a4hw"][0], DS["a4hw"][1])

            esAO.close()
            tok_a_done = (DS["a4aw"][0], DS["a4aw"][1])
            tok_hq_done = (DS["a4hw"][0], DS["a4hw"][1])

            if "a" in dbg_outs:
                @block.gpsimd
                def _(g):
                    WD(g, tok_a_done)
                    ddma("dbg", g.dma_start(dbg_outs["a"], a_d))
            if "hq" in dbg_outs:
                @block.gpsimd
                def _(g):
                    WD(g, tok_hq_done)
                    ddma("dbg", g.dma_start(dbg_outs["hq"], act_d))

            if stage < 5:
                _finish(block, SEM, C, W, DS)
                return nc, dbg_outs

            for engsec in ("tensor", "vector", "scalar", "gpsimd"):
                def _bar(e):
                    WD(e, tok_a_done)
                    WD(e, tok_hq_done)
                getattr(block, engsec)(_bar)

            esQ2 = ExitStack()
            esA.callback(esQ2.close)
            sbQ2 = lambda n, s, d: esQ2.enter_context(nc.sbuf_tensor(n, s, d))
            q2_fm = sbQ2("q2_fm", [128, NCH * T], BF16)
            k2_fm = sbQ2("k2_fm", [128, NCH * T], BF16)
            v2_tm = sbQ2("v2_tm", [128, NT * D], BF16)

            with nc.sbuf_tensor("hq_fm", [128, NCH * T], BF16) as hq_fm, \
                 nc.sbuf_tensor("wtile2", [128, 2 * 1024], BF16) as wtile2, \
                 nc.sbuf_tensor("wv2_sb", [128, NCH * 512], BF16) as wv2_sb, \
                 nc.sbuf_tensor("sq_st", [128, 2 * 512], F32) as sq_st, \
                 nc.sbuf_tensor("rn_sb", [16, 2 * T], F32) as rn_sb, \
                 nc.sbuf_tensor("oh2", [16, NCH * 128], F32) as oh2, \
                 nc.psum_tensor("b1_pa", [128, 512], F32) as b1_pa, \
                 nc.psum_tensor("b1_pb", [128, 512], F32) as b1_pb, \
                 nc.psum_tensor("b1_ss", [16, 4 * 512], F32) as b1_ss:
                ps_p2 = [b1_pa, b1_pb]

                _bar = (C["ts"], C["vs"], C["ss"])

                @block.gpsimd
                def _(g):
                    W(g, "ts", _bar[0])
                    W(g, "vs", _bar[1])
                    W(g, "ss", _bar[2])
                    ins = g.memset(oh2[:], 1.0)
                    tk = inc("gs", ins)
                    g.wait_ge(SEM["gs"], tk)
                    ins = None
                    for c in range(NCH):
                        ins = g.affine_select(out=oh2[:, c * 128:(c + 1) * 128],
                                              in_=oh2[:, c * 128:(c + 1) * 128],
                                              pattern=[[-1, 2], [0, 64]],
                                              compare_op=ALU.is_equal, fill=0.0,
                                              base=-2 * c, channel_multiplier=1)
                    inc("gs", ins)
                tok_oh2 = C["gs"]

                _hqfm = []

                @block.sync
                def _(sy):
                    WD(sy, tok_hq_done)
                    for c in range(NCH):
                        _hqfm.append(ddma("hqfm", sy.dma_start(
                            hq_fm[:, c * T:(c + 1) * T],
                            act_d[:, c * 128:(c + 1) * 128], transpose=True)))
                tok_hqfm = _hqfm[-1]

                tok_wt2 = [None] * 16
                tok_mm2 = {}
                tok_evac2 = {}
                tok_sq = {}
                tok_emm = {}
                tok_rnsq = {}
                for dc in range(16):
                    wsrc = wq_d if dc < 8 else wk_d
                    cc = dc % 8

                    @block.gpsimd
                    def _(g, dc=dc, wsrc=wsrc, cc=cc):
                        if dc >= 2:
                            W(g, "ts", tok_mm2[(dc - 2, 3)])
                        src = wsrc.rearrange("(dk p) f -> p dk f", p=128)[
                            :, :, cc * 128:(cc + 1) * 128]
                        dst = wtile2[:, (dc % 2) * 1024:(dc % 2 + 1) * 1024] \
                            .rearrange("p (dk f) -> p dk f", f=128)
                        tok_wt2[dc] = ddma("w2t%d" % (dc % 2), g.dma_start(dst, src))

                    for tq in range(4):
                        u = dc * 4 + tq

                        @block.tensor
                        def _(t, dc=dc, tq=tq, u=u, cc=cc):
                            WD(t, tok_wt2[dc])
                            if u == 0:
                                WD(t, tok_hqfm)
                            if u >= 2:
                                W(t, "ss", tok_sq[u - 2])
                                en, tkv = tok_evac2[u - 2]
                                W(t, en, tkv)
                            ins = None
                            for dk in range(NCH):
                                ins = t.matmul(
                                    ps_p2[u % 2][:],
                                    wtile2[:, (dc % 2) * 1024 + dk * 128:
                                           (dc % 2) * 1024 + (dk + 1) * 128],
                                    hq_fm[:, dk * T + tq * 512:dk * T + tq * 512 + 512],
                                    start=(dk == 0), stop=(dk == NCH - 1))
                            tok_mm2[(dc, tq)] = inc("ts", ins)
                            if u >= 1 and u != 32:
                                up = u - 1
                                dcp, tqp = up // 4, up % 4
                                W(t, "ss", tok_sq[up])
                                if up >= 32 and 0 in tok_rnsq:
                                    W(t, "ss", tok_rnsq[0])
                                ins = t.matmul(
                                    b1_ss[0:16, tqp * 512:(tqp + 1) * 512],
                                    e_tiles[:, (dcp % 8) * 16:(dcp % 8) * 16 + 16],
                                    sq_st[:, (up % 2) * 512:(up % 2 + 1) * 512],
                                    start=((dcp % 8) == 0), stop=((dcp % 8) == 7),
                                    skip_group_check=True)
                                tok_emm[up] = inc("ts", ins)

                        @block.scalar
                        def _(s, dc=dc, tq=tq, u=u, cc=cc):
                            W(s, "ts", tok_mm2[(dc, tq)])
                            if u >= 2:
                                W(s, "ts", tok_emm.get(u - 2, 0))
                            ins = s.activation(sq_st[:, (u % 2) * 512:(u % 2 + 1) * 512],
                                               ps_p2[u % 2][:], AF.Square)
                            tok_sq[u] = inc("ss", ins)

                        if dc < 8:
                            @block.vector
                            def _(v, dc=dc, tq=tq, u=u, cc=cc):
                                W(v, "ts", tok_mm2[(dc, tq)])
                                dstf = q2_fm[:, cc * T + tq * 512:cc * T + tq * 512 + 512]
                                ins = v.tensor_copy(dstf, ps_p2[u % 2][:])
                                tok_evac2[u] = ("vs", inc("vs", ins))

                        if dc >= 8:
                            @block.vector
                            def _(v, dc=dc, tq=tq, u=u, cc=cc):
                                W(v, "ts", tok_mm2[(dc, tq)])
                                dstf = k2_fm[:, cc * T + tq * 512:cc * T + tq * 512 + 512]
                                ins = v.tensor_scalar(dstf, ps_p2[u % 2][:],
                                                      wqk_fm[:, cc:cc + 1], None,
                                                      ALU.mult)
                                tok_evac2[u] = ("vs", inc("vs", ins))

                    if dc == 7 or dc == 15:
                        qk = 0 if dc == 7 else 1

                        @block.tensor
                        def _(t, u=dc * 4 + 3):
                            W(t, "ss", tok_sq[u])
                            if 0 in tok_rnsq:
                                W(t, "ss", tok_rnsq[0])
                            dcp, tqp = u // 4, u % 4
                            ins = t.matmul(
                                b1_ss[0:16, tqp * 512:(tqp + 1) * 512],
                                e_tiles[:, (dcp % 8) * 16:(dcp % 8) * 16 + 16],
                                sq_st[:, (u % 2) * 512:(u % 2 + 1) * 512],
                                start=((dcp % 8) == 0), stop=((dcp % 8) == 7),
                                skip_group_check=True)
                            tok_emm[u] = inc("ts", ins)
                        tok_e_last = C["ts"]

                        @block.scalar
                        def _(s, qk=qk, tok_e_last=tok_e_last):
                            W(s, "ts", tok_e_last)
                            ins = s.activation(rn_sb[0:16, qk * T:(qk + 1) * T],
                                               b1_ss[0:16, 0:2048], AF.Sqrt,
                                               bias=smalls[0:16, 34:35])
                            inc("ss", ins)
                        tok_rn_sqrt = C["ss"]
                        tok_rnsq[qk] = tok_rn_sqrt

                        @block.vector
                        def _(v, qk=qk, tok_rn_sqrt=tok_rn_sqrt):
                            W(v, "ss", tok_rn_sqrt)
                            ins = v.reciprocal(rn_sb[0:16, qk * T:(qk + 1) * T],
                                               rn_sb[0:16, qk * T:(qk + 1) * T])
                            tk = inc("vs", ins)
                            if qk == 0:
                                v.wait_ge(SEM["vs"], tk)
                                ins = v.tensor_scalar(
                                    rn_sb[0:16, 0:T], rn_sb[0:16, 0:T],
                                    smalls[0:16, 35:36], None, ALU.mult)
                                inc("vs", ins)
                        tok_rn = {"q": 0, "k": 0}
                    if dc == 7:
                        tok_rn_q = C["vs"]
                    if dc == 15:
                        tok_rn_k = C["vs"]

                tok_bcmm = {}
                tok_bcmul = {}
                for qk in range(2):
                    src_fm = q2_fm if qk == 0 else k2_fm
                    tok_rn_x = tok_rn_q if qk == 0 else tok_rn_k
                    for c in range(NCH):
                        for tq in range(4):
                            u = qk * 32 + c * 4 + tq

                            @block.tensor
                            def _(t, qk=qk, c=c, tq=tq, u=u, tok_rn_x=tok_rn_x):
                                W(t, "vs", tok_rn_x)
                                if u < 2:
                                    W(t, "gs", tok_oh2)
                                    for up in (62, 63):
                                        en, tkv = tok_evac2[up]
                                        W(t, en, tkv)
                                        W(t, "ss", tok_sq[up])
                                if u >= 2:
                                    W(t, "vs", tok_bcmul[u - 2])
                                ins = t.matmul(
                                    ps_p2[u % 2][:],
                                    oh2[:, c * 128:(c + 1) * 128],
                                    rn_sb[0:16, qk * T + tq * 512:qk * T + tq * 512 + 512],
                                    start=True, stop=True, skip_group_check=True)
                                tok_bcmm[u] = inc("ts", ins)

                            @block.vector
                            def _(v, qk=qk, c=c, tq=tq, u=u, src_fm=src_fm):
                                W(v, "ts", tok_bcmm[u])
                                if qk == 0:
                                    en, tkv = tok_evac2[c * 4 + tq]
                                    W(v, en, tkv)
                                sl_ = src_fm[:, c * T + tq * 512:c * T + tq * 512 + 512]
                                ins = v.tensor_tensor(sl_, sl_, ps_p2[u % 2][:],
                                                      ALU.mult)
                                tok_bcmul[u] = inc("vs", ins)
                tok_qk2_done = C["vs"]

                tok_mm_v2 = {}
                tok_evac_v2 = {}
                for dh2 in range(2):
                    @block.gpsimd
                    def _(g, dh2=dh2):
                        if dh2 == 1:
                            W(g, "ts", tok_mm_v2[NT - 1])
                        for dk in range(NCH):
                            ddma("wv2%d" % dh2, g.dma_start(
                                wv2_sb[:, dk * 512:(dk + 1) * 512],
                                wv_d[dk * 128:(dk + 1) * 128,
                                     dh2 * 512:dh2 * 512 + 512]))
                    tok_wv2 = (DS["wv2%d" % dh2][0], DS["wv2%d" % dh2][1])
                    for tt in range(NT):
                        u = dh2 * NT + tt

                        @block.tensor
                        def _(t, tt=tt, dh2=dh2, u=u, tok_wv2=tok_wv2):
                            WD(t, tok_wv2)
                            if u == 0:
                                W(t, "vs", tok_qk2_done)
                            if u >= 2:
                                W(t, "vs", tok_evac_v2[u - 2])
                            ins = None
                            for dk in range(NCH):
                                ins = t.matmul(
                                    ps_p2[u % 2][:],
                                    hq_fm[:, dk * T + tt * 128:dk * T + (tt + 1) * 128],
                                    wv2_sb[:, dk * 512:(dk + 1) * 512],
                                    start=(dk == 0), stop=(dk == NCH - 1))
                            tok_mm_v2[u] = inc("ts", ins)

                        @block.vector
                        def _(v, tt=tt, dh2=dh2, u=u):
                            W(v, "ts", tok_mm_v2[u])
                            ins = v.tensor_copy(
                                v2_tm[:, tt * 1024 + dh2 * 512:tt * 1024 + dh2 * 512 + 512],
                                ps_p2[u % 2][:])
                            tok_evac_v2[u] = inc("vs", ins)
                tok_v2_done = C["vs"]

            for nm, src in (("q2", q2_fm), ("k2", k2_fm), ("v2", v2_tm)):
                if nm in dbg_outs:
                    @block.gpsimd
                    def _(g, nm=nm, src=src):
                        W(g, "vs", tok_v2_done)
                        ddma("dbg", g.dma_start(dbg_outs[nm], src[:]))

            ao2_fm = sbQ2("ao2_fm", [128, NCH * T], BF16)

            if stage < 6:
                _finish(block, SEM, C, W, DS)
                return nc, dbg_outs

            tok_pv2_done, tok_att2_ts = emit_attention(
                nc, block, H_, smalls, q2_fm, k2_fm, v2_tm,
                ao2_fm, probs_d, None, None, far_c, mc_add, use_rel=False,
                tag="b2",
                bg_chunks=[(w1b_d[e, rb * 128:(rb + 1) * 128,
                                  cb * 1024:(cb + 1) * 1024],
                            w1_d[e, rb * 128:(rb + 1) * 128,
                                 cb * 1024:(cb + 1) * 1024])
                           for e in range(E) for rb in range(8)
                           for cb in range(4)],
                first_waits_mm=[("vs", tok_v2_done), ("ss", C["ss"])],
                nheads=ATTN_NHEADS)

            if "ao2" in dbg_outs:
                @block.gpsimd
                def _(g):
                    W(g, "vs", tok_pv2_done)
                    ddma("dbg", g.dma_start(dbg_outs["ao2"], ao2_fm[:]))

            if stage < 7:
                _finish(block, SEM, C, W, DS)
                return nc, dbg_outs

            GB = 4
            with nc.sbuf_tensor("wo_sb", [128, NCH * 1024], BF16) as wo_sb, \
                 nc.sbuf_tensor("b3_o", [128, GB * D], F32) as o3_st, \
                 nc.sbuf_tensor("b3_h2", [128, GB * D], BF16) as h23_st, \
                 nc.sbuf_tensor("b3_bn", [128, GB * 12], F32) as bn3, \
                 nc.sbuf_tensor("b3_mv", [128, GB * 2], F32) as mv3, \
                 nc.psum_tensor("b3_p0", [128, 1024], F32) as b3p0, \
                 nc.psum_tensor("b3_p1", [128, 1024], F32) as b3p1:
                ps3 = [b3p0, b3p1]

                @block.gpsimd
                def _(g):
                    W(g, "ts", tok_att2_ts)
                    for rr in range(4):
                        nm = "b2pw%d" % rr
                        if nm in DS:
                            g.wait_ge(DS[nm][0], DS[nm][1])
                    for dk in range(NCH):
                        ddma("wos", g.dma_start(
                            wo_sb[:, dk * 1024:(dk + 1) * 1024],
                            wo_d[dk * 128:(dk + 1) * 128, :]))
                tok_wos = (DS["wos"][0], DS["wos"][1])

                tok_oev3 = {}
                tok_h23 = {}
                tok_sd3b = {}
                tok_aggr3b = {}
                tok_amr3_last = 0
                tok_hw3_grp = None
                for g0 in range(0, NT, GB):
                    tiles = list(range(g0, g0 + GB))
                    for i, tt in enumerate(tiles):
                        tok_mm3_box = []

                        @block.tensor
                        def _(t, i=i, tt=tt, g0=g0):
                            if g0 == 0 and i == 0:
                                WD(t, tok_wos)
                                W(t, "vs", tok_pv2_done)
                            if (g0 + i) >= 2:
                                W(t, "vs", tok_oev3[g0 + i - 2])
                            ins = None
                            for dh2 in range(2):
                                for dk in range(NCH):
                                    ins = t.matmul(
                                        ps3[i % 2][:, dh2 * 512:dh2 * 512 + 512],
                                        ao2_fm[:, dk * T + tt * 128:dk * T + (tt + 1) * 128],
                                        wo_sb[:, dk * 1024 + dh2 * 512:
                                              dk * 1024 + dh2 * 512 + 512],
                                        start=(dk == 0), stop=(dk == NCH - 1),
                                        skip_group_check=True)
                            tok_mm3_box.append(inc("ts", ins))

                        @block.vector
                        def _(v, i=i, g0=g0):
                            W(v, "ts", tok_mm3_box[0])
                            if (g0 + i) >= GB:
                                W(v, "vs", tok_h23.get(g0 + i - GB, 0))
                            ins2 = v.tensor_copy(o3_st[:, i * D:(i + 1) * D],
                                                 ps3[i % 2][:])
                            tok_oev3[g0 + i] = inc("vs", ins2)

                    @block.vector
                    def _(v, g0=g0):
                        for i in range(GB):
                            W(v, "vs", tok_oev3[g0 + i])
                            v.bn_stats(bn3[:, i * 12:i * 12 + 6],
                                       o3_st[:, i * D:i * D + 512])
                            ins = v.bn_stats(bn3[:, i * 12 + 6:i * 12 + 12],
                                             o3_st[:, i * D + 512:i * D + 1024])
                            tk = inc("vs", ins)
                            v.wait_ge(SEM["vs"], tk)
                            ins = v.bn_aggr(mv3[:, i * 2:i * 2 + 2],
                                            bn3[:, i * 12:i * 12 + 12])
                            tok_aggr3b[(g0, i)] = inc("vs", ins)

                    @block.scalar
                    def _(s, g0=g0):
                        for i in range(GB):
                            W(s, "vs", tok_aggr3b[(g0, i)])
                            ins = s.activation(smalls[:, 16 + i:17 + i],
                                               mv3[:, i * 2 + 1:i * 2 + 2],
                                               AF.Sqrt, bias=smalls[:, 33:34])
                            tok_sd3b[(g0, i)] = inc("ss", ins)

                    @block.vector
                    def _(v, g0=g0, pv_amr=tok_amr3_last, pv_hw=tok_hw3_grp):
                        for i in range(GB):
                            sd = smalls[:, 16 + i:17 + i]
                            rstd = smalls[:, 20 + i:21 + i]
                            nmr = smalls[:, 24 + i:25 + i]
                            W(v, "ss", tok_sd3b[(g0, i)])
                            ins = v.reciprocal(rstd, sd)
                            tk = inc("vs", ins)
                            v.wait_ge(SEM["vs"], tk)
                            ins = v.tensor_scalar(nmr, rstd, mv3[:, i * 2:i * 2 + 1],
                                                  -1.0, ALU.mult, ALU.mult)
                            tk = inc("vs", ins)
                            v.wait_ge(SEM["vs"], tk)
                            ins = v.tensor_scalar(o3_st[:, i * D:(i + 1) * D],
                                                  o3_st[:, i * D:(i + 1) * D],
                                                  rstd, nmr, ALU.mult, ALU.add)
                            tk = inc("vs", ins)
                            v.wait_ge(SEM["vs"], tk)
                            ins = v.tensor_tensor(o3_st[:, i * D:(i + 1) * D],
                                                  o3_st[:, i * D:(i + 1) * D],
                                                  g_bc[:], ALU.mult)
                            tk = inc("vs", ins)
                            v.wait_ge(SEM["vs"], tk)
                            if i == 0 and pv_hw is not None:
                                WD(v, pv_hw)
                            ins = v.tensor_tensor(h23_st[:, i * D:(i + 1) * D],
                                                  o3_st[:, i * D:(i + 1) * D],
                                                  b_bc[:], ALU.add)
                            tok_h23[g0 + i] = inc("vs", ins)
                    tok_amr3_last = tok_h23[(g0 + GB - 1)]

                    @block.sync
                    def _(sy, tiles=tiles, g0=g0):
                        for i, tt in enumerate(tiles):
                            W(sy, "vs", tok_h23[g0 + i])
                            if g0 == 0 and i == 0:
                                WD(sy, (DS["hqfm"][0], DS["hqfm"][1]))
                            ddma("b3hw", sy.dma_start(
                                act_d[tt * 128:(tt + 1) * 128, :],
                                h23_st[:, i * D:(i + 1) * D]))
                    tok_hw3_grp = (DS["b3hw"][0], DS["b3hw"][1])
                    if g0 == 4:
                        tok_h2_half = (DS["b3hw"][0], DS["b3hw"][1])

            tok_h2_done = (DS["b3hw"][0], DS["b3hw"][1])
            esQ2.close()

            if "h2" in dbg_outs:
                @block.gpsimd
                def _(g):
                    WD(g, tok_h2_done)
                    ddma("dbg", g.dma_start(dbg_outs["h2"], act_d))

            if stage < 8:
                _finish(block, SEM, C, W, DS)
                return nc, dbg_outs

            tok_wcv = (DS["wcv"][0], DS["wcv"][1])
            emit_moe(nc, block, H_, DS, smalls, ident, g_bc, b_bc, gb_bc, b1_fm,
                     b2_sb, gw_d, w1b_d, w2b_d, act_d, a_d, y_d, tok_h2_done,
                     tok_wcv, tok_h2_half)

            _finish(block, SEM, C, W, DS)
            return nc, dbg_outs

    return nc, dbg_outs


def emit_attention(nc, block, H_, smalls, q_fm, k_fm, v_tm,
                   ao_fm, probs_dram, dtl, ntl, far_c, mc_add, use_rel, tag,
                   first_waits_mm, nheads=H, first_waits_v=(), bg_chunks=None):
    inc, W, ddma, WD = H_["inc"], H_["W"], H_["ddma"], H_["WD"]
    SEM, C = H_["SEM"], H_["C"]
    _es = ExitStack()
    probs = _es.enter_context(nc.sbuf_tensor(tag + "_probs", [128, 4 * T], BF16))
    strips = _es.enter_context(nc.sbuf_tensor(tag + "_strips", [128, 28 * 512], BF16))
    psA = _es.enter_context(nc.psum_tensor(tag + "_psA", [128, 2048], F32))
    psB = _es.enter_context(nc.psum_tensor(tag + "_psB", [128, 1536], F32))
    ps_pv = _es.enter_context(nc.psum_tensor(tag + "_pv", [64, 512], F32))
    NSTR = 28
    NU = 16 * nheads

    tok_exp_buf = {"A": 0, "B": 0}
    probs_slot_tok = [None] * 4
    strip_owner_ts = [0] * NSTR
    strip_load = {}
    strip_rr = [0]
    tok_pv_mm = {}
    tok_pv_evac = [0]
    tok_mm, tok_ebias, tok_exp, tok_vadd = {}, {}, {}, {}
    tok_recip, tok_norm, tok_probs_w = {}, {}, {}

    def hp_of(u):
        return (u // 16, u % 16)

    def buf_of(pos):
        return (psA, "A") if QT_BUF[pos] == "A" else (psB, "B")

    def emit_pv_mms(t, hp, qg):
        smax = 4 * qg + 3
        needs = {}
        for sc in range(smax + 1):
            semh, val = strip_load[(hp, qg, sc)][1]
            needs[id(semh)] = (semh, max(needs.get(id(semh), (semh, 0))[1], val))
        for semh, val in needs.values():
            t.wait_ge(semh, val)
        W(t, "vs", tok_pv_evac[0])
        ins = None
        for qt in range(4 * qg, 4 * qg + 4):
            ql = qt - 4 * qg
            for sc in range(qt + 1):
                slot = strip_load[(hp, qg, sc)][0]
                ins = t.matmul(ps_pv[0:64, ql * 128:(ql + 1) * 128],
                               v_tm[:, sc * 1024 + hp * 64:sc * 1024 + hp * 64 + 64],
                               strips[:, slot * 512 + ql * 128:slot * 512 + (ql + 1) * 128],
                               start=(sc == 0), stop=(sc == qt),
                               skip_group_check=True)
        tok = inc("ts", ins)
        tok_pv_mm[(hp, qg)] = tok
        for sc in range(smax + 1):
            strip_owner_ts[strip_load[(hp, qg, sc)][0]] = tok

    def emit_pv_evac(v, hp, qg):
        W(v, "ts", tok_pv_mm[(hp, qg)])
        ins = v.tensor_copy(
            ao_fm[64 * (hp % 2):64 * (hp % 2) + 64,
                  (hp // 2) * T + qg * 512:(hp // 2) * T + (qg + 1) * 512],
            ps_pv[0:64, 0:512])
        tok_pv_evac[0] = inc("vs", ins)

    def emit_recip(v, u):
        hh, pos = hp_of(u)
        r = pos % 4
        sln = 128 * (QT_ORDER[pos] + 1)
        W(v, "ss", tok_exp[(hh, pos)])
        if u >= 4:
            W(v, "vs", tok_norm[hp_of(u - 4)])
            W(v, "vs", tok_recip[hp_of(u - 4)])
        ins = v.reciprocal(smalls[:, 12 + r:13 + r], smalls[:, 8 + r:9 + r])
        tk = inc("vs", ins)
        tok_recip[(hh, pos)] = tk
        v.wait_ge(SEM["vs"], tk)
        ins = v.tensor_scalar(probs[:, r * T:r * T + sln],
                              probs[:, r * T:r * T + sln],
                              smalls[:, 12 + r:13 + r], None, ALU.mult)
        tok_norm[(hh, pos)] = inc("vs", ins)

    def emit_write(g, u):
        hh, pos = hp_of(u)
        qt = QT_ORDER[pos]
        r = pos % 4
        sln = 128 * (qt + 1)
        W(g, "vs", tok_norm[(hh, pos)])
        ins = g.dma_start(probs_dram[hh, qt * 128:(qt + 1) * 128, 0:sln],
                          probs[:, r * T:r * T + sln])
        tok_probs_w[(hh, pos)] = ddma("%spw%d" % (tag, r), ins)
        probs_slot_tok[r] = tok_probs_w[(hh, pos)]

    def emit_strips_qg(sy, hh, qg, sc_lo=0, sc_hi=None):
        wtoks = [tok_probs_w[(hh, QT_ORDER.index(qt))]
                 for qt in range(4 * qg, 4 * qg + 4)]
        if sc_hi is None:
            sc_hi = 4 * qg + 4
        for sc in range(sc_lo, sc_hi):
            slot = strip_rr[0] % NSTR
            strip_rr[0] += 1
            for wt in wtoks:
                WD(sy, wt)
            W(sy, "ts", strip_owner_ts[slot])
            q0 = max(qg * 512, sc * 128)
            off = q0 - qg * 512
            nm = "%sst%02d" % (tag, slot % 14)
            if nm in H_["DS"]:
                WD(sy, (H_["DS"][nm][0], H_["DS"][nm][1]))
            ins = sy.dma_start(
                strips[:, slot * 512 + off:(slot + 1) * 512],
                probs_dram[hh, q0:(qg + 1) * 512, sc * 128:(sc + 1) * 128],
                transpose=True)
            strip_load[(hh, qg, sc)] = (slot, ddma(nm, ins))

    for u in range(NU):
        h, pos = hp_of(u)
        qt = QT_ORDER[pos]
        buf, bname = buf_of(pos)
        sl = 128 * (qt + 1)
        r = pos % 4

        STRIP_AT = {2: (3, 0, 6), 3: (3, 6, 11), 4: (3, 11, 16),
                    5: (2, 0, 6), 6: (2, 6, 12),
                    8: (1, 0, 4), 9: (1, 4, 8),
                    12: (0, 0, 4)}
        if pos in STRIP_AT and h >= 1 and "strips" not in ATTN_SKIP:
            @block.sync
            def _(sy, h=h, sched=STRIP_AT[pos]):
                qg, lo, hi = sched
                emit_strips_qg(sy, h - 1, qg, lo, hi)

        @block.tensor
        def _(t, h=h, pos=pos, qt=qt, buf=buf, bname=bname, sl=sl, u=u):
            if u == 0:
                for sem_n, tk in first_waits_mm:
                    W(t, sem_n, tk)
            W(t, "ss", tok_exp_buf[bname])
            ins = None
            for j in range(0, sl, 512):
                n = min(512, sl - j)
                ins = t.matmul(
                    buf[:, j:j + n],
                    q_fm[64 * (h % 2):64 * (h % 2) + 64,
                         (h // 2) * T + qt * 128:(h // 2) * T + (qt + 1) * 128],
                    k_fm[64 * (h % 2):64 * (h % 2) + 64,
                         (h // 2) * T + j:(h // 2) * T + j + n],
                    start=True, stop=True, skip_group_check=True)
            tok_mm[(h, pos)] = inc("ts", ins)
            if pos in PV_AT_POS and h >= 1 and "pv" not in ATTN_SKIP:
                emit_pv_mms(t, h - 1, PV_AT_POS[pos])

        @block.vector
        def _(v, h=h, pos=pos, qt=qt, buf=buf, bname=bname, sl=sl, r=r, u=u):
            if u == 0:
                for sem_n, tk in first_waits_v:
                    W(v, sem_n, tk)
            W(v, "ts", tok_mm[(h, pos)])
            if use_rel:
                ins = v.tensor_tensor(buf[:, qt * 128:(qt + 1) * 128],
                                      buf[:, qt * 128:(qt + 1) * 128],
                                      dtl[:, h * 128:(h + 1) * 128], ALU.add)
                if qt >= 1:
                    ins = v.tensor_tensor(buf[:, (qt - 1) * 128:qt * 128],
                                          buf[:, (qt - 1) * 128:qt * 128],
                                          ntl[:, h * 128:(h + 1) * 128], ALU.add)
            else:
                ins = v.tensor_tensor(buf[:, qt * 128:(qt + 1) * 128],
                                      buf[:, qt * 128:(qt + 1) * 128],
                                      mc_add[:], ALU.add)
            tk = inc("vs", ins)
            v.wait_ge(SEM["vs"], tk)
            if u >= 4:
                W(v, "ss", tok_exp[hp_of(u - 4)])
                W(v, "vs", tok_ebias[hp_of(u - 4)])
            ins = v.reduce_max(smalls[:, r:r + 1], buf[0:128, 0:sl],
                               axis=AX.X, negate=True)
            tk = inc("vs", ins)
            if use_rel:
                v.wait_ge(SEM["vs"], tk)
                ins = v.tensor_tensor(smalls[:, 4 + r:5 + r], smalls[:, r:r + 1],
                                      far_c[:, h:h + 1], ALU.add)
                tk = inc("vs", ins)
            tok_ebias[(h, pos)] = tk
            if u >= 2 and "norm" not in ATTN_SKIP:
                emit_recip(v, u - 2)
            if pos in PV_AT_POS and h >= 1 and "pv" not in ATTN_SKIP:
                emit_pv_evac(v, h - 1, PV_AT_POS[pos])

        if u % 2 == 1:
            @block.scalar
            def _(s, u=u):
                for uu in (u - 1, u):
                    hh, pp = hp_of(uu)
                    qq = QT_ORDER[pp]
                    rr = pp % 4
                    ssl = 128 * (qq + 1)
                    bufq, bnq = buf_of(pp)
                    eb = (smalls[:, 4 + rr:5 + rr] if use_rel
                          else smalls[:, rr:rr + 1])
                    W(s, "vs", tok_ebias[(hh, pp)])
                    WD(s, probs_slot_tok[rr])
                    if uu >= 4:
                        W(s, "ss", tok_exp[hp_of(uu - 4)])
                        W(s, "vs", tok_norm[hp_of(uu - 4)])
                    ins = s.activation(probs[:, rr * T:rr * T + ssl],
                                       bufq[0:128, 0:ssl], AF.Exp, bias=eb,
                                       accum_out=smalls[:, 8 + rr:9 + rr])
                    tok_exp[(hh, pp)] = inc("ss", ins)
                    tok_exp_buf[bnq] = tok_exp[(hh, pp)]

        if u >= 3 and "writes" not in ATTN_SKIP and "norm" not in ATTN_SKIP:
            @block.gpsimd
            def _(g, u=u):
                emit_write(g, u - 3)

        if bg_chunks:
            @block.gpsimd
            def _(g):
                dst, src = bg_chunks.pop(0)
                ddma("wcv", g.dma_start(dst, src))


    if "norm" not in ATTN_SKIP:
        for u in (NU - 2, NU - 1):
            @block.vector
            def _(v, u=u):
                emit_recip(v, u)

    if "writes" not in ATTN_SKIP and "norm" not in ATTN_SKIP:
        @block.gpsimd
        def _(g):
            for u in (NU - 3, NU - 2, NU - 1):
                emit_write(g, u)

    if "strips" not in ATTN_SKIP and "pv" not in ATTN_SKIP:
        hp = nheads - 1
        for qg in (3, 2, 1, 0):
            @block.sync
            def _(sy, qg=qg):
                emit_strips_qg(sy, hp, qg)

            @block.tensor
            def _(t, qg=qg):
                emit_pv_mms(t, hp, qg)

            @block.vector
            def _(v, qg=qg):
                emit_pv_evac(v, hp, qg)
    tok_last_ts = C["ts"]

    _es.close()
    return tok_pv_evac[0], tok_last_ts


def emit_bias_build(nc, block, H_, DS, tok_win, win_d, dtl, ntl, far_c, mc_add,
                    flip, bigd_r, bign_r, tok_p0_done, tok_const, tok_hw_last):
    inc, W, ddma, WD = H_["inc"], H_["W"], H_["ddma"], H_["WD"]
    SEM, C = H_["SEM"], H_["C"]
    vs_snap, ss_snap = C["vs"], C["ss"]
    with nc.psum_tensor("ps_fl", [128, 512], F32) as ps_fl:

        @block.gpsimd
        def _(g):
            WD(g, tok_win)
            WD(g, tok_hw_last[0])
            WD(g, tok_hw_last[1])
            W(g, "vs", vs_snap)
            W(g, "ss", ss_snap)
            ddma("bias", g.dma_start(bigd_r[:], win_d[129:257, :]))
            ddma("bias", g.dma_start(bign_r[:], win_d[1:129, :]))
        tok_bias_dma = (DS["bias"][0], DS["bias"][1])

        @block.vector
        def _(v):
            WD(v, tok_bias_dma)
            W(v, "vs", tok_p0_done)
            ins = None
            for h in range(H):
                ins = v.tensor_scalar(dtl[:, h * 128:(h + 1) * 128],
                                      bigd_r[:, h::16], far_c[:, h:h + 1],
                                      None, ALU.subtract)
                ins = v.tensor_scalar(ntl[:, h * 128:(h + 1) * 128],
                                      bign_r[:, h::16], far_c[:, h:h + 1],
                                      None, ALU.subtract)
            inc("vs", ins)
        tok_gather = C["vs"]

        tok_flev = 0
        for i, buf in enumerate((dtl, ntl)):
            for j in range(4):
                @block.tensor
                def _(t, i=i, j=j, buf=buf, tok_flev=tok_flev):
                    if i == 0 and j == 0:
                        W(t, "vs", tok_gather)
                        W(t, "gs", tok_const)
                    else:
                        W(t, "vs", tok_flev)
                    ins = t.matmul(ps_fl[:], flip[:],
                                   buf[:, j * 512:(j + 1) * 512],
                                   start=True, stop=True,
                                   skip_group_check=True)
                    inc("ts", ins)
                tok_flmm = C["ts"]

                @block.vector
                def _(v, i=i, j=j, buf=buf, tok_flmm=tok_flmm):
                    W(v, "ts", tok_flmm)
                    if i == 0:
                        ins = None
                        for q in range(4):
                            ins = v.tensor_tensor(
                                buf[:, j * 512 + q * 128:
                                    j * 512 + (q + 1) * 128],
                                ps_fl[:, q * 128:(q + 1) * 128],
                                mc_add[:], ALU.add)
                    else:
                        ins = v.tensor_copy(
                            buf[:, j * 512:(j + 1) * 512], ps_fl[:])
                    inc("vs", ins)
                tok_flev = C["vs"]
    return C["vs"]


def emit_ln_to_fm(nc, block, H_, smalls, src_rows, fm_dst, dram_buf, tag,
                  first_waits_e=(), first_waits_d=(), fm_waits=()):
    inc, W, ddma, WD = H_["inc"], H_["W"], H_["ddma"], H_["WD"]
    SEM, C = H_["SEM"], H_["C"]
    D_ = D
    _esl = ExitStack()
    x_st = _esl.enter_context(nc.sbuf_tensor(tag + "_x", [128, 2 * D_], F32))
    o_st = _esl.enter_context(nc.sbuf_tensor(tag + "_o", [128, 2 * D_], BF16))
    bn_st = _esl.enter_context(nc.sbuf_tensor(tag + "_bn", [128, 2 * 12], F32))
    mv_st = _esl.enter_context(nc.sbuf_tensor(tag + "_mv", [128, 2 * 2], F32))
    tok_x = [None] * NT
    tok_apply = [0] * NT
    tok_bn = [0] * NT
    tok_sd = [0] * NT
    tok_w = [None] * NT
    for tt in range(NT):
        r = tt % 2

        @block.sync
        def _(sy, tt=tt, r=r):
            if tt == 0:
                for sem_n, tk in first_waits_e:
                    W(sy, sem_n, tk)
                for tk in first_waits_d:
                    WD(sy, tk)
            if tt >= 2:
                W(sy, "vs", tok_apply[tt - 2])
            ins = sy.dma_start(x_st[:, r * D_:(r + 1) * D_],
                               src_rows[tt * 128:(tt + 1) * 128, :])
            tok_x[tt] = ddma("%sx%d" % (tag, r), ins)

        @block.vector
        def _(v, tt=tt, r=r):
            WD(v, tok_x[tt])
            v.bn_stats(bn_st[:, r * 12:r * 12 + 6], x_st[:, r * D_:r * D_ + 512])
            ins = v.bn_stats(bn_st[:, r * 12 + 6:r * 12 + 12],
                             x_st[:, r * D_ + 512:r * D_ + 1024])
            tk = inc("vs", ins)
            v.wait_ge(SEM["vs"], tk)
            ins = v.bn_aggr(mv_st[:, r * 2:r * 2 + 2], bn_st[:, r * 12:r * 12 + 12])
            tok_bn[tt] = inc("vs", ins)

        if tt % 2 == 1:
            @block.scalar
            def _(s, tt=tt):
                for uu in (tt - 1, tt):
                    ru = uu % 2
                    W(s, "vs", tok_bn[uu])
                    if uu >= 2:
                        W(s, "vs", tok_apply[uu - 2])
                    ins = s.activation(smalls[:, 16 + ru:17 + ru],
                                       mv_st[:, ru * 2 + 1:ru * 2 + 2], AF.Sqrt,
                                       bias=smalls[:, 33:34])
                    tok_sd[uu] = inc("ss", ins)

            for uu in (tt - 1, tt):
                @block.vector
                def _(v, uu=uu):
                    ru = uu % 2
                    sd = smalls[:, 16 + ru:17 + ru]
                    rstd = smalls[:, 18 + ru:19 + ru]
                    nmr = smalls[:, 20 + ru:21 + ru]
                    W(v, "ss", tok_sd[uu])
                    ins = v.reciprocal(rstd, sd)
                    tk = inc("vs", ins)
                    v.wait_ge(SEM["vs"], tk)
                    ins = v.tensor_scalar(nmr, rstd,
                                          mv_st[:, ru * 2:ru * 2 + 1], -1.0,
                                          ALU.mult, ALU.mult)
                    tk = inc("vs", ins)
                    v.wait_ge(SEM["vs"], tk)
                    if uu >= 2:
                        WD(v, tok_w[uu - 2])
                    ins = v.tensor_scalar(o_st[:, ru * D_:(ru + 1) * D_],
                                          x_st[:, ru * D_:(ru + 1) * D_],
                                          rstd, nmr, ALU.mult, ALU.add)
                    tok_apply[uu] = inc("vs", ins)

                @block.sync
                def _(sy, uu=uu):
                    ru = uu % 2
                    W(sy, "vs", tok_apply[uu])
                    ins = sy.dma_start(dram_buf[uu * 128:(uu + 1) * 128, :],
                                       o_st[:, ru * D_:(ru + 1) * D_])
                    tok_w[uu] = ddma("%sw%d" % (tag, ru), ins)

    _fm = []

    @block.sync
    def _(sy):
        for sem_n, tk in fm_waits:
            W(sy, sem_n, tk)
        WD(sy, tok_w[NT - 2])
        WD(sy, tok_w[NT - 1])
        for c in range(NCH):
            _fm.append(ddma("%sfm" % tag,
                            sy.dma_start(fm_dst[:, c * T:(c + 1) * T],
                                         dram_buf[:, c * 128:(c + 1) * 128],
                                         transpose=True)))
    _esl.close()
    return _fm[-1], (tok_w[NT - 2], tok_w[NT - 1])


def _finish(block, SEM, C, W, DS):
    @block.sync
    def _(sy):
        for s in ("ts", "vs", "ss", "gs"):
            W(sy, s, C[s])
        for name, (h, v) in DS.items():
            if v > 0:
                sy.wait_ge(h, v)


def _shard_inputs(inputs):
    x = np.ascontiguousarray(inputs["L0_attn_x"], dtype=np.float32)
    common = {
        "wqkv": inputs["L0_attn_Wqkv"], "bqkv": inputs["L0_attn_bqkv"],
        "wout": inputs["L0_attn_Wout"], "bout": inputs["L0_attn_bout"],
        "rel": inputs["L0_attn_rel_emb"], "lng": inputs["L0_ffn_ln_g"],
        "lnb": inputs["L0_ffn_ln_b"], "wq": inputs["L0_ffn_Wq"],
        "wk": inputs["L0_ffn_Wk"], "wv": inputs["L0_ffn_Wv"],
        "qsc": inputs["L0_ffn_q_scale"], "ksc": inputs["L0_ffn_k_scale"],
        "wo": inputs["L0_ffn_Wo"], "gw": inputs["L0_ffn_gate_w"],
        "gb": inputs["L0_ffn_gate_b"], "w1": inputs["L0_ffn_w1"],
        "b1": inputs["L0_ffn_b1"], "w2": inputs["L0_ffn_w2"],
        "b2": inputs["L0_ffn_b2"],
    }
    common = {k: np.ascontiguousarray(v, dtype=np.float32) for k, v in common.items()}
    return [dict(common, x=np.ascontiguousarray(x[b])) for b in range(8)]


def kernel(**inputs):
    nc, _ = build_program(stage=99)
    in_maps = _shard_inputs(inputs)
    res = run_bass_kernel_spmd(nc, in_maps, core_ids=list(range(8)))
    return np.stack([r["y"] for r in res.results], axis=0)


def emit_moe(nc, block, H_, DS, smalls, ident, g_bc, b_bc, gb_bc, b1_fm, b2_sb,
             gw_d, w1_d, w2_d, act_d, a_d, y_d, tok_h2_done, tok_wcv,
             tok_h2_half):
    inc, W, ddma, WD = H_["inc"], H_["W"], H_["ddma"], H_["WD"]
    SEM, C = H_["SEM"], H_["C"]
    HT = 1024
    _es = ExitStack()
    sb = lambda n, sh, d: _es.enter_context(nc.sbuf_tensor(n, sh, d))
    h2f = sb("h2f", [128, NCH * HT], BF16)
    w2_sb = sb("w2_sb", [128, 32 * 1024], BF16)
    hid = sb("hid", [128, 16 * 1024], BF16)
    o_half = sb("o_half", [128, 8 * 1024], F32)
    w1t = sb("w1t", [128, 2 * 1024], BF16)
    gw_sb = sb("gw_sb", [128, 64], BF16)
    gtm = sb("gtm", [128, 64], F32)
    t6_st = sb("b6_t", [128, 2 * D], F32)
    a6_st = sb("b6_a", [128, 2 * D], F32)
    bn6 = sb("b6_bn", [128, 2 * 12], F32)
    mv6 = sb("b6_mv", [128, 2 * 2], F32)
    ps_o5 = [_es.enter_context(nc.psum_tensor("b5_o0", [128, 1024], F32)),
             _es.enter_context(nc.psum_tensor("b5_o1", [128, 1024], F32))]

    @block.gpsimd
    def _(g):
        WD(g, tok_h2_done)
        for dk in range(NCH):
            ddma("gws", g.dma_start(gw_sb[:, dk * 8:(dk + 1) * 8],
                                    gw_d[dk * 128:(dk + 1) * 128, :]))
    tok_gws = (DS["gws"][0], DS["gws"][1])

    st = {"w2last": 0, "ywgrp": None, "tok_yw": {}, "wcv": tok_wcv}
    tok_w1mm = {}
    tok_hid_ev = {}
    tok_stt = {}
    tok_a6 = {}

    for hf in range(2):
        st["b6_t6tt"] = {}
        st["b6_hook"] = (lambda tt, hf=hf: emit_moe_b6_tt(
            nc, block, H_, DS, smalls, g_bc, b_bc, o_half, t6_st, a6_st,
            bn6, mv6, a_d, y_d, hf, st, tok_stt, tok_a6, tt))
        tok_h2f = emit_moe_h2f(nc, block, H_, DS, h2f, act_d, hf, st,
                               tok_h2_half if hf == 0 else tok_h2_done)
        with ExitStack() as es_g2:
            gtr = es_g2.enter_context(nc.sbuf_tensor("gtr%d" % hf,
                                                     [8, 1024], F32))
            gtn = es_g2.enter_context(nc.sbuf_tensor("gtn%d" % hf,
                                                     [8, 1024], F32))
            with nc.psum_tensor("b5_g%d" % hf, [8, 1024], F32) as b5g:
                tok_gtn, tok_gtm_done = emit_moe_gate(
                    nc, block, H_, DS, smalls, ident, gb_bc, h2f, gw_sb, gtm,
                    gtr, gtn, ps_o5, b5g, hf, tok_gws, tok_h2f)
            tok_oinit = emit_moe_b2init(nc, block, H_, gtn, b2_sb, o_half,
                                        ps_o5, tok_gtn)
        with ExitStack() as es_h:
            ps_h5 = [es_h.enter_context(
                nc.psum_tensor("b5_h%d_%d" % (hf, i), [128, 512], F32))
                for i in range(4)]
            for e in range(E):
                emit_moe_expert(nc, block, H_, DS, b1_fm, h2f, w2_sb, hid,
                                o_half, w1t, gtm, ps_h5, ps_o5, w1_d, w2_d, e,
                                hf, st, tok_w1mm, tok_hid_ev, tok_stt,
                                tok_oinit)
    _es.close()


def emit_moe_h2f(nc, block, H_, DS, h2f, act_d, hf, st, tok_h2_done):
    ddma, WD, W = H_["ddma"], H_["WD"], H_["W"]
    HT = 1024
    _t = []

    @block.sync
    def _(sy):
        WD(sy, tok_h2_done)
        if hf == 1:
            W(sy, "ts", st.get("h2f_free", st["w2last"]))
        for c in range(NCH):
            _t.append(ddma("h2f%d" % hf, sy.dma_start(
                h2f[:, c * HT:(c + 1) * HT],
                act_d[hf * HT:(hf + 1) * HT, c * 128:(c + 1) * 128],
                transpose=True)))
    return _t[-1]


def emit_moe_gate(nc, block, H_, DS, smalls, ident, gb_bc, h2f, gw_sb, gtm, gtr,
                  gtn, ps_o5, b5g, hf, tok_gws, tok_h2f):
    inc, W, ddma, WD = H_["inc"], H_["W"], H_["ddma"], H_["WD"]
    SEM, C = H_["SEM"], H_["C"]
    HT = 1024

    @block.tensor
    def _(t):
        WD(t, tok_gws)
        WD(t, tok_h2f)
        ins = None
        for j in range(2):
            for dk in range(NCH):
                ins = t.matmul(
                    b5g[0:8, j * 512:(j + 1) * 512],
                    gw_sb[:, dk * 8:(dk + 1) * 8],
                    h2f[:, dk * HT + j * 512:dk * HT + j * 512 + 512],
                    start=(dk == 0), stop=(dk == NCH - 1),
                    skip_group_check=True)
        inc("ts", ins)
    tok_gmm = C["ts"]

    @block.vector
    def _(v):
        W(v, "ts", tok_gmm)
        ins = v.tensor_copy(gtr[0:8, :], b5g[0:8, :])
        inc("vs", ins)
    tok_gtr = C["vs"]

    tok_gtile = {}
    for tt in range(8):
        @block.tensor
        def _(t, tt=tt):
            W(t, "vs", tok_gtr)
            if tt >= 2:
                W(t, "vs", tok_gtile[tt - 2])
            ins = t.matmul(ps_o5[tt % 2][:, 0:8],
                           gtr[0:8, tt * 128:(tt + 1) * 128],
                           ident[0:8, 0:8], start=True, stop=True,
                           is_transpose=True, skip_group_check=True)
            inc("ts", ins)
        tok_gt1 = C["ts"]

        @block.vector
        def _(v, tt=tt, tok_gt1=tok_gt1):
            W(v, "ts", tok_gt1)
            ins = v.tensor_tensor(gtm[:, tt * 8:(tt + 1) * 8],
                                  ps_o5[tt % 2][:, 0:8], gb_bc[:], ALU.add)
            tk = inc("vs", ins)
            v.wait_ge(SEM["vs"], tk)
            ins = v.reduce_max(smalls[:, 0:1], gtm[:, tt * 8:(tt + 1) * 8],
                               axis=AX.X, negate=True)
            inc("vs", ins)
        tok_gmx = C["vs"]

        @block.scalar
        def _(s, tt=tt, tok_gmx=tok_gmx):
            W(s, "vs", tok_gmx)
            ins = s.activation(gtm[:, tt * 8:(tt + 1) * 8],
                               gtm[:, tt * 8:(tt + 1) * 8],
                               AF.Exp, bias=smalls[:, 0:1],
                               accum_out=smalls[:, 8:9])
            inc("ss", ins)
        tok_gex = C["ss"]

        @block.vector
        def _(v, tt=tt, tok_gex=tok_gex):
            W(v, "ss", tok_gex)
            ins = v.reciprocal(smalls[:, 12:13], smalls[:, 8:9])
            tk = inc("vs", ins)
            v.wait_ge(SEM["vs"], tk)
            ins = v.tensor_scalar(gtm[:, tt * 8:(tt + 1) * 8],
                                  gtm[:, tt * 8:(tt + 1) * 8],
                                  smalls[:, 12:13], None, ALU.mult)
            tok_gtile[tt] = inc("vs", ins)

        @block.tensor
        def _(t, tt=tt):
            W(t, "vs", tok_gtile[tt])
            ins = t.matmul(b5g[0:8, tt * 128:(tt + 1) * 128],
                           gtm[:, tt * 8:(tt + 1) * 8], ident[:],
                           start=True, stop=True, is_transpose=True,
                           skip_group_check=True)
            inc("ts", ins)
        tok_gt2 = C["ts"]

        @block.vector
        def _(v, tt=tt, tok_gt2=tok_gt2):
            W(v, "ts", tok_gt2)
            ins = v.tensor_copy(gtn[0:8, tt * 128:(tt + 1) * 128],
                                b5g[0:8, tt * 128:(tt + 1) * 128])
            inc("vs", ins)
    return C["vs"], tok_gtile


def emit_moe_b2init(nc, block, H_, gtn, b2_sb, o_half, ps_o5, tok_gtn):
    inc, W = H_["inc"], H_["W"]
    SEM, C = H_["SEM"], H_["C"]
    tok_b2i = {}
    for tt in range(8):
        @block.tensor
        def _(t, tt=tt):
            W(t, "vs", tok_gtn)
            if tt >= 2:
                W(t, "vs", tok_b2i[tt - 2])
            ins = None
            for dh2 in range(2):
                ins = t.matmul(ps_o5[tt % 2][:, dh2 * 512:(dh2 + 1) * 512],
                               gtn[0:8, tt * 128:(tt + 1) * 128],
                               b2_sb[0:8, dh2 * 512:(dh2 + 1) * 512],
                               start=True, stop=True, skip_group_check=True)
            inc("ts", ins)
        tok_bi = C["ts"]

        @block.vector
        def _(v, tt=tt, tok_bi=tok_bi):
            W(v, "ts", tok_bi)
            ins = v.tensor_copy(o_half[:, tt * 1024:(tt + 1) * 1024],
                                ps_o5[tt % 2][:])
            tok_b2i[tt] = inc("vs", ins)
    return C["vs"]


def emit_moe_expert(nc, block, H_, DS, b1_fm, h2f, w2_sb, hid, o_half,
                    w1t, gtm, ps_h5, ps_o5, w1_d, w2_d, e, hf, st, tok_w1mm,
                    tok_hid_ev, tok_stt, tok_oinit):
    inc, W, ddma, WD = H_["inc"], H_["W"], H_["ddma"], H_["WD"]
    SEM, C = H_["SEM"], H_["C"]
    HT = 1024

    @block.scalar
    def _(sc):
        W(sc, "ts", st["w2last"])
        WD(sc, st["wcv"])
        for ffc in range(0, 32, 2):
            ddma("w2l0", sc.dma_start(w2_sb[:, ffc * 1024:(ffc + 1) * 1024],
                                      w2_d[e, ffc * 128:(ffc + 1) * 128, :]))

    @block.sync
    def _(sy):
        W(sy, "ts", st["w2last"])
        WD(sy, st["wcv"])
        for ffc in range(1, 32, 2):
            ddma("w2l1", sy.dma_start(w2_sb[:, ffc * 1024:(ffc + 1) * 1024],
                                      w2_d[e, ffc * 128:(ffc + 1) * 128, :]))
    tok_w2l = [(DS["w2l0"][0], DS["w2l0"][1]), (DS["w2l1"][0], DS["w2l1"][1])]

    for fh in range(2):
        for ffl in range(16):
            ffc = fh * 16 + ffl
            slot = ffc % 2

            @block.gpsimd
            def _(g, ffc=ffc, slot=slot):
                if e == 0 and hf == 0 and ffc == 0:
                    WD(g, st["wcv"])
                if slot in tok_w1mm:
                    W(g, "ts", tok_w1mm[slot])
                src = w1_d.rearrange("e (dk p) f -> e p dk f", p=128)[
                    e][:, :, ffc * 128:(ffc + 1) * 128]
                dst = w1t[:, slot * 1024:(slot + 1) * 1024] \
                    .rearrange("p (dk f) -> p dk f", f=128)
                ddma("w1t%d" % slot, g.dma_start(dst, src))
            tok_w1l = (DS["w1t%d" % slot][0], DS["w1t%d" % slot][1])

            for th in range(2):
                u5 = ffl * 2 + th

                @block.tensor
                def _(t, ffl=ffl, th=th, slot=slot, u5=u5, tok_w1l=tok_w1l, fh=fh):
                    WD(t, tok_w1l)
                    prev = (tok_hid_ev.get(u5 - 4, 0) if u5 >= 4
                            else tok_hid_ev.get(28 + u5, 0))
                    W(t, "ss", prev)
                    ins = None
                    for dk in range(NCH):
                        ins = t.matmul(
                            ps_h5[u5 % 4][:],
                            w1t[:, slot * 1024 + dk * 128:slot * 1024 + (dk + 1) * 128],
                            h2f[:, dk * HT + th * 512:dk * HT + th * 512 + 512],
                            start=(dk == 0), stop=(dk == NCH - 1))
                    tok_w1mm[slot] = inc("ts", ins)
                    tok_w1mm[(e, hf, fh, ffl, th)] = tok_w1mm[slot]

                if u5 % 4 == 3:
                    @block.scalar
                    def _(s, u5=u5, fh=fh):
                        for uu in range(u5 - 3, u5 + 1):
                            fl2, th2 = uu // 2, uu % 2
                            W(s, "ts", tok_w1mm[(e, hf, fh, fl2, th2)])
                            ffc2 = fh * 16 + fl2
                            ins = s.activation(
                                hid[:, fl2 * 1024 + th2 * 512:
                                    fl2 * 1024 + th2 * 512 + 512],
                                ps_h5[uu % 4][:], AF.Gelu_apprx_tanh,
                                bias=b1_fm[:, e * 32 + ffc2:e * 32 + ffc2 + 1])
                            tok_hid_ev[uu] = inc("ss", ins)

        tok_hid_all = C["ss"]
        for tt in range(8):
            @block.tensor
            def _(t, tt=tt, fh=fh, tok_hid_all=tok_hid_all):
                W(t, "ss", tok_hid_all)
                WD(t, tok_w2l[0])
                WD(t, tok_w2l[1])
                if tt >= 2:
                    W(t, "vs", tok_stt[tt - 2])
                else:
                    if e == 0 and fh == 0:
                        W(t, "vs", tok_oinit)
                    for pp in (6, 7):
                        W(t, "vs", tok_stt.get(pp, 0))
                ins = None
                for ffl in range(16):
                    ffc2 = fh * 16 + ffl
                    for dh2 in range(2):
                        ins = t.matmul(
                            ps_o5[tt % 2][:, dh2 * 512:(dh2 + 1) * 512],
                            hid[:, ffl * 1024 + tt * 128:ffl * 1024 + (tt + 1) * 128],
                            w2_sb[:, ffc2 * 1024 + dh2 * 512:
                                  ffc2 * 1024 + dh2 * 512 + 512],
                            start=(ffl == 0), stop=(ffl == 15),
                            skip_group_check=True)
                tok_stt[(e, hf, fh, tt, "mm")] = inc("ts", ins)

            @block.vector
            def _(v, tt=tt, fh=fh):
                W(v, "ts", tok_stt[(e, hf, fh, tt, "mm")])
                if e == 0 and fh == 0:
                    W(v, "vs", tok_oinit)
                ins = v.scalar_tensor_tensor(
                    o_half[:, tt * 1024:(tt + 1) * 1024],
                    ps_o5[tt % 2][:],
                    gtm[:, tt * 8 + e:tt * 8 + e + 1],
                    o_half[:, tt * 1024:(tt + 1) * 1024],
                    ALU.mult, ALU.add)
                tok_stt[tt] = inc("vs", ins)

            if e == E - 1 and fh == 1:
                st["b6_hook"](tt)
        st["w2last"] = tok_stt[(e, hf, fh, 7, "mm")]
        if e == E - 1 and fh == 1:
            st["h2f_free"] = tok_w1mm[(e, hf, 1, 15, 1)]


def emit_moe_b6_tt(nc, block, H_, DS, smalls, g_bc, b_bc, o_half, t6_st,
                   a6_st, bn6, mv6, a_d, y_d, hf, st, tok_stt, tok_a6, tt):
    inc, W, ddma, WD = H_["inc"], H_["W"], H_["ddma"], H_["WD"]
    SEM, C = H_["SEM"], H_["C"]
    tok_t6tt = st["b6_t6tt"]
    if True:
        r6 = tt % 2
        sd6 = smalls[:, 16 + r6:17 + r6]
        rstd6 = smalls[:, 20 + r6:21 + r6]
        nmr6 = smalls[:, 24 + r6:25 + r6]

        @block.vector
        def _(v, tt=tt, r6=r6):
            W(v, "vs", tok_stt[tt])
            if tt >= 2:
                W(v, "vs", tok_t6tt.get(tt - 2, 0))
            v.bn_stats(bn6[:, r6 * 12:r6 * 12 + 6],
                       o_half[:, tt * 1024:tt * 1024 + 512])
            ins = v.bn_stats(bn6[:, r6 * 12 + 6:r6 * 12 + 12],
                             o_half[:, tt * 1024 + 512:tt * 1024 + 1024])
            tk = inc("vs", ins)
            v.wait_ge(SEM["vs"], tk)
            ins = v.bn_aggr(mv6[:, r6 * 2:r6 * 2 + 2], bn6[:, r6 * 12:r6 * 12 + 12])
            inc("vs", ins)
        tok_bn6 = C["vs"]

        @block.scalar
        def _(s, tt=tt, r6=r6, tok_bn6=tok_bn6, sd6=sd6):
            W(s, "vs", tok_bn6)
            ins = s.activation(sd6, mv6[:, r6 * 2 + 1:r6 * 2 + 2], AF.Sqrt,
                               bias=smalls[:, 33:34])
            inc("ss", ins)
        tok_sd6 = C["ss"]

        @block.vector
        def _(v, tt=tt, r6=r6, tok_sd6=tok_sd6, sd6=sd6, rstd6=rstd6, nmr6=nmr6):
            W(v, "ss", tok_sd6)
            ins = v.reciprocal(rstd6, sd6)
            tk = inc("vs", ins)
            v.wait_ge(SEM["vs"], tk)
            ins = v.tensor_scalar(nmr6, rstd6, mv6[:, r6 * 2:r6 * 2 + 1],
                                  -1.0, ALU.mult, ALU.mult)
            tk = inc("vs", ins)
            v.wait_ge(SEM["vs"], tk)
            ins = v.tensor_scalar(t6_st[:, r6 * D:(r6 + 1) * D],
                                  o_half[:, tt * 1024:(tt + 1) * 1024],
                                  rstd6, nmr6, ALU.mult, ALU.add)
            tk = inc("vs", ins)
            v.wait_ge(SEM["vs"], tk)
            ins = v.tensor_tensor(t6_st[:, r6 * D:(r6 + 1) * D],
                                  t6_st[:, r6 * D:(r6 + 1) * D],
                                  g_bc[:], ALU.mult)
            tk = inc("vs", ins)
            v.wait_ge(SEM["vs"], tk)
            ins = v.tensor_tensor(t6_st[:, r6 * D:(r6 + 1) * D],
                                  t6_st[:, r6 * D:(r6 + 1) * D], b_bc[:], ALU.add)
            tk = inc("vs", ins)
            v.wait_ge(SEM["vs"], tk)
            v.bn_stats(bn6[:, r6 * 12:r6 * 12 + 6], t6_st[:, r6 * D:r6 * D + 512])
            ins = v.bn_stats(bn6[:, r6 * 12 + 6:r6 * 12 + 12],
                             t6_st[:, r6 * D + 512:r6 * D + 1024])
            tk = inc("vs", ins)
            v.wait_ge(SEM["vs"], tk)
            ins = v.bn_aggr(mv6[:, r6 * 2:r6 * 2 + 2], bn6[:, r6 * 12:r6 * 12 + 12])
            tok_t6tt[tt] = inc("vs", ins)

        @block.sync
        def _(sy, tt=tt, r6=r6, hf=hf):
            if tt >= 2:
                W(sy, "vs", tok_a6.get(tt - 2, 0))
            WD(sy, st["tok_yw"].get(r6))
            ddma("b6a", sy.dma_start(
                a6_st[:, r6 * D:(r6 + 1) * D],
                a_d[(hf * 8 + tt) * 128:(hf * 8 + tt + 1) * 128, :]))
        tok_ald = (DS["b6a"][0], DS["b6a"][1])

        @block.scalar
        def _(s, tt=tt, r6=r6, sd6=sd6):
            W(s, "vs", tok_t6tt[tt])
            ins = s.activation(sd6, mv6[:, r6 * 2 + 1:r6 * 2 + 2], AF.Sqrt,
                               bias=smalls[:, 33:34])
            inc("ss", ins)
        tok_sd6b = C["ss"]

        @block.vector
        def _(v, tt=tt, r6=r6, tok_sd6b=tok_sd6b, tok_ald=tok_ald, sd6=sd6,
              rstd6=rstd6, nmr6=nmr6):
            W(v, "ss", tok_sd6b)
            ins = v.reciprocal(rstd6, sd6)
            tk = inc("vs", ins)
            v.wait_ge(SEM["vs"], tk)
            ins = v.tensor_scalar(nmr6, rstd6, mv6[:, r6 * 2:r6 * 2 + 1],
                                  -1.0, ALU.mult, ALU.mult)
            tk = inc("vs", ins)
            v.wait_ge(SEM["vs"], tk)
            WD(v, tok_ald)
            if st["ywgrp"] is not None:
                WD(v, st["ywgrp"])
            ins = v.tensor_scalar(t6_st[:, r6 * D:(r6 + 1) * D],
                                  t6_st[:, r6 * D:(r6 + 1) * D],
                                  rstd6, nmr6, ALU.mult, ALU.add)
            tk = inc("vs", ins)
            v.wait_ge(SEM["vs"], tk)
            ins = v.tensor_tensor(a6_st[:, r6 * D:(r6 + 1) * D],
                                  t6_st[:, r6 * D:(r6 + 1) * D],
                                  a6_st[:, r6 * D:(r6 + 1) * D], ALU.add)
            tok_a6[tt] = inc("vs", ins)

        @block.sync
        def _(sy, tt=tt, r6=r6, hf=hf):
            W(sy, "vs", tok_a6[tt])
            tok = ddma("yw", sy.dma_start(
                y_d[(hf * 8 + tt) * 128:(hf * 8 + tt + 1) * 128, :],
                a6_st[:, r6 * D:(r6 + 1) * D]))
            st["tok_yw"][r6] = tok
        st["ywgrp"] = (DS["yw"][0], DS["yw"][1])
